# version 01
# baseline (speedup 1.0000x reference)
import numpy as np
import concourse.bass as bass
import concourse.mybir as mybir
from concourse.bass_utils import run_bass_kernel_spmd
from concourse.bass_core_rust import IndirectOffsetOnAxis
from contextlib import ExitStack

F32 = mybir.dt.float32
BF16 = mybir.dt.bfloat16
I32 = mybir.dt.int32
AF = mybir.ActivationFunctionType
ALU = mybir.AluOpType
AX = mybir.AxisListType

NCORES = 8
DM = 4096
QL = 1024
NH = 32
HD = 128
NKV = 8
NCH = 64
NK = DM // 128
NM = DM // 128


class Sem:
    def __init__(self, name):
        self.name = name
        self.n = 0
        self.h = None

    def plan(self, k):
        self.n += k
        return self.n


class Plan:

    def __init__(self):
        self.ops = {"sync": [], "gpsimd": [], "vector": [], "scalar": [],
                    "tensor": []}

    def emit(self, eng, f):
        self.ops[eng].append(f)

    @staticmethod
    def _waits(e, waits):
        for (s, v) in waits:
            if v > 0:
                e.wait_ge(s.h, v)

    def dma(self, eng, sem, out, in_, waits=(), **kw):
        waits = tuple(waits)
        m = sem.plan(16)

        def f(e):
            self._waits(e, waits)
            e.dma_start(out, in_, **kw).then_inc(sem.h, 16)
        self.emit(eng, f)
        return m

    def idma(self, sem, out, in_, idx, waits=()):
        waits = tuple(waits)
        m = sem.plan(16)

        def f(e):
            self._waits(e, waits)
            e.indirect_dma_start(out, None, in_, IndirectOffsetOnAxis(idx, 0)
                                 ).then_inc(sem.h, 16)
        self.emit("gpsimd", f)
        return m

    def dmatp(self, sem, out, in_, waits=()):
        waits = tuple(waits)
        m = sem.plan(16)

        def f(e):
            self._waits(e, waits)
            e.dma_start_transpose(out, in_).then_inc(sem.h, 16)
        self.emit("scalar", f)
        return m

    def chained(self, eng, sem, build, waits=()):
        waits = tuple(waits) + ((sem, sem.n),)
        m = sem.plan(1)

        def f(e):
            self._waits(e, waits)
            build(e).then_inc(sem.h, 1)
        self.emit(eng, f)
        return m

    def top(self, build, waits=(), inc=None):
        waits = tuple(waits)
        m = inc.plan(1) if inc is not None else None

        def f(e):
            self._waits(e, waits)
            ins = build(e)
            if inc is not None:
                ins.then_inc(inc.h, 1)
        self.emit("tensor", f)
        return m

    def fence(self, sem, bank_ap, zc, zr, waits=()):
        return self.top(lambda e: e.matmul(bank_ap, zc, zr, start=False,
                                           stop=True, skip_group_check=True),
                        waits=waits, inc=sem)

    def barrier(self, sems, engines=("sync", "gpsimd", "vector", "scalar",
                                     "tensor")):
        snap = [(s, s.n) for s in sems]
        for eng in engines:
            def f(e, snap=snap):
                self._waits(e, snap)
            self.emit(eng, f)


def build_nc(debug=False):
    nc = bass.Bass("TRN2", target_bir_lowering=False)

    xT = nc.dram_tensor("xT", [DM, QL], F32, kind="ExternalInput").ap()
    wq = nc.dram_tensor("wq", [DM, DM], F32, kind="ExternalInput").ap()
    wo = nc.dram_tensor("wo", [DM, DM], F32, kind="ExternalInput").ap()
    wbT = nc.dram_tensor("wbT", [DM, DM], F32, kind="ExternalInput").ap()
    kvf = nc.dram_tensor("kvf", [262144, 128], F32, kind="ExternalInput").ap()
    bt = nc.dram_tensor("bt", [512], I32, kind="ExternalInput").ap()
    atT = nc.dram_tensor("atT", [DM, 128], F32, kind="ExternalInput").ap()
    bp = nc.dram_tensor("bp", [128, DM], F32, kind="ExternalInput").ap()
    wrT = nc.dram_tensor("wrT", [DM, 8], F32, kind="ExternalInput").ap()
    bb = nc.dram_tensor("bb", [DM], F32, kind="ExternalInput").ap()
    cosH = nc.dram_tensor("cosT", [128, QL], F32, kind="ExternalInput").ap()
    sinH = nc.dram_tensor("sinT", [128, QL], F32, kind="ExternalInput").ap()
    e16H = nc.dram_tensor("e16", [8, 128], F32, kind="ExternalInput").ap()
    ovecH = nc.dram_tensor("ovec", [128, 1], F32, kind="ExternalInput").ap()
    yT = nc.dram_tensor("yT", [DM, QL], F32, kind="ExternalOutput").ap()

    kind = "ExternalOutput" if debug else "Internal"
    qhbm = nc.dram_tensor("qhbm", [NH, 128, QL], BF16, kind=kind).ap()
    s1h = nc.dram_tensor("s1h", [NM, 128, QL], F32, kind=kind).ap()
    s2h = nc.dram_tensor("s2h", [NM, 128, QL], F32, kind=kind).ap()
    x1bh = nc.dram_tensor("x1bh", [NK, 128, QL], BF16, kind=kind).ap()

    P = Plan()
    sHW = Sem("sHW")
    sGQ = Sem("sGQ")
    sSD = Sem("sSD")
    sV = Sem("sV")
    sS = Sem("sS")
    sT = Sem("sT")
    sems = [sHW, sGQ, sSD, sV, sS, sT]

    with ExitStack() as root:
        sb = lambda n, s, d: root.enter_context(nc.sbuf_tensor(n, s, d))
        psalloc = lambda n, s: root.enter_context(nc.psum_tensor(n, s, F32))

        psA = psalloc("psA", [128, 1024])
        psB = psalloc("psB", [128, 1024])
        psC = psalloc("psC", [128, 1024])
        psD = psalloc("psD", [128, 512])
        psE = psalloc("psE", [128, 512])

        identf = sb("identf", [128, 128], F32)
        e16s = sb("e16sb", [8, 128], F32)
        ovec = sb("ovecs", [128, 1], F32)
        ones_bf = sb("ones_bf", [128, 1], BF16)
        onesf = sb("onesf", [1, 128], F32)
        cst = sb("cst", [1, 8], F32)
        IOT = sb("IOT", [128, 128], I32)
        btL = sb("btL", [64, 8], I32)
        btLf = sb("btLf", [64, 8], F32)
        btTs = sb("btTs", [8, 64], F32)
        base_sb = sb("base_sb", [128, 64], F32)
        idx_all = sb("idx_all", [128, 1024], I32)
        r1 = sb("r1", [1, 1024], F32)
        r2 = sb("r2", [1, 1024], F32)
        r3 = sb("r3", [1, 1024], F32)
        mu_b = sb("mu_b", [128, 1024], F32)
        rstd_b = sb("rstd_b", [128, 1024], F32)
        b_sb = sb("b_sb", [128, 32], F32)
        m1s = sb("m1s", [128, 1], F32)
        m2s = sb("m2s", [128, 1], F32)
        d21 = sb("d21", [128, 1], F32)
        eds = sb("eds", [128, 1], F32)
        w1s = sb("w1s", [128, 1], F32)
        w2s = sb("w2s", [128, 1], F32)
        eq1 = sb("eq1", [128, 8], F32)
        eq2 = sb("eq2", [128, 8], F32)
        t8 = sb("t8", [128, 8], F32)
        t8b = sb("t8b", [128, 8], F32)
        zc_bf = sb("zc_bf", [1, 128], BF16)
        zr_bf = sb("zr_bf", [1, 8], BF16)
        FW = []

        ctx13 = root.enter_context(ExitStack())
        ctxbf = ctx13.enter_context(
            nc.sbuf_tensor("ctxbf", [128, NK * 1024], BF16))
        with ExitStack() as ph12:
            sb12 = lambda n, s, d: ph12.enter_context(nc.sbuf_tensor(n, s, d))
            qga = sb12("qga", [128, 8 * 1024], BF16)
            KT0 = sb12("KT0", [128, NCH * 128], BF16)
            KT1 = sb12("KT1", [128, NCH * 128], BF16)
            V0 = sb12("V0", [128, NCH * 128], BF16)
            V1 = sb12("V1", [128, NCH * 128], BF16)
            KTb = [KT0, KT1]
            Vb = [V0, V1]
            Kst = [sb12("Kst0", [128, 128], BF16),
                   sb12("Kst1", [128, 128], BF16),
                   sb12("Kst2", [128, 128], BF16),
                   sb12("Kst3", [128, 128], BF16)]

            P.dma("sync", sHW, btL[:, :], bt)
            mW_tab = P.dma("sync", sHW, e16s[:], e16H)
            P.dma("sync", sHW, ovec[:], ovecH)
            for m in range(32):
                P.dma("sync", sHW, b_sb[:, m:m + 1], bb[m * 128:(m + 1) * 128])
            mHW_ph0 = sHW.n

            P.chained("gpsimd", sGQ, lambda e: e.iota(
                IOT[:], pattern=[[1, 128]], base=0, channel_multiplier=-1))
            mGQ_iota = sGQ.n

            P.chained("vector", sV, lambda e: e.memset(zc_bf[:], 0.0))
            mV_z = P.chained("vector", sV, lambda e: e.memset(zr_bf[:], 0.0))
            FW.append((sV, mV_z))
            fence = lambda bank, waits=(): P.fence(
                sT, bank, zc_bf[:], zr_bf[0:1, 0:1],
                waits=tuple(waits) + tuple(FW))
            mV = P.chained("vector", sV, lambda e: e.tensor_single_scalar(
                identf[:], IOT[:], 0.0, ALU.is_equal),
                waits=[(sGQ, mGQ_iota)])
            P.chained("vector", sV, lambda e: e.memset(ones_bf[:], 1.0))
            P.chained("vector", sV, lambda e: e.memset(onesf[:], 1.0))
            P.chained("vector", sV, lambda e: e.memset(cst[0:1, 0:1], 1e-5))
            P.chained("vector", sV, lambda e: e.memset(cst[0:1, 1:2], -0.5))
            P.chained("vector", sV, lambda e: e.memset(cst[0:1, 2:3], -1.0))
            mV_btf = P.chained("vector", sV, lambda e: e.tensor_copy(
                btLf[:], btL[:]), waits=[(sHW, 16)])

            P.top(lambda e: e.transpose(psE[0:8, 0:64], btLf[:],
                                        identf[0:64, 0:64]),
                  waits=[(sV, mV_btf)])
            mT = fence(psE[:, 300:301])
            mV_btT = P.chained("vector", sV, lambda e: e.tensor_copy(
                btTs[:], psE[0:8, 0:64]), waits=[(sT, mT)])
            P.top(lambda e: e.matmul(psD[:, 0:64], e16s[:],
                                     btTs[:], start=True, stop=True),
                  waits=[(sV, mV_btT), (sHW, mW_tab)])
            mT_idx = fence(psD[:, 300:301])
            mV_base = P.chained("vector", sV, lambda e: e.tensor_scalar(
                base_sb[:], psD[:, 0:64], 256.0, ovec[:, 0:1],
                ALU.mult, ALU.add), waits=[(sT, mT_idx)])
            mV_idx = 0
            mV_base_snap = mV_base
            for g in range(8):
                for kv in range(2):
                    col = (g * 2 + kv) * 64
                    off = float(kv * 128 + g * 16)
                    mV_idx = P.chained(
                        "vector", sV,
                        lambda e, col=col, off=off: e.tensor_single_scalar(
                            idx_all[:, col:col + 64], base_sb[:], off,
                            ALU.add))

            kst_free = [(sV, 0)] * 4
            kbuf_free = [(sV, 0), (sV, 0)]
            gather_done = [None] * 8
            ktp_done = [None] * 8
            kst_i = [0]

            def plan_gather_slot(g, slot):
                buf = g % 2
                c = slot // 2
                if slot % 2 == 0:
                    ks = kst_i[0] % 4
                    kst_i[0] += 1
                    mK = P.idma(sGQ, Kst[ks][:], kvf,
                                idx_all[:, (g * 2) * 64 + c:(g * 2) * 64 + c + 1],
                                waits=[kst_free[ks], kbuf_free[buf],
                                       (sV, mV_idx)])
                    mtp = P.dmatp(sSD, KTb[buf][:, c * 128:(c + 1) * 128],
                                  Kst[ks][:], waits=[(sGQ, mK)])
                    kst_free[ks] = (sSD, mtp)
                else:
                    P.idma(sGQ, Vb[buf][:, c * 128:(c + 1) * 128], kvf,
                           idx_all[:, (g * 2 + 1) * 64 + c:(g * 2 + 1) * 64 + c + 1],
                           waits=[kbuf_free[buf], (sV, mV_idx)])
                if slot == 127:
                    gather_done[g] = (sGQ, sGQ.n)
                    ktp_done[g] = (sSD, sSD.n)

            with ExitStack() as ph1:
                sb1 = lambda n, s, d: ph1.enter_context(
                    nc.sbuf_tensor(n, s, d))
                cosT = sb1("cosTs", [128, QL], F32)
                sinT = sb1("sinTs", [128, QL], F32)
                Xr = [sb1("Xr%d" % i, [128, 1024], BF16) for i in range(2)]
                Wqr = [sb1("Wqr%d" % i, [128, 512], BF16) for i in range(2)]
                qraw = [sb1("qraw%d" % i, [128, 1024], F32) for i in range(2)]
                shf = sb1("shf", [128, 1024], F32)
                t1r = sb1("t1r", [128, 1024], F32)
                qbt = [sb1("qbt%d" % i, [128, 1024], BF16) for i in range(2)]

                mHW_cos = P.dma("sync", sHW, cosT[:], cosH)
                mHW_sin = P.dma("sync", sHW, sinT[:], sinH)

                xr_free = [(sV, 0)] * 2
                wq_free = [(sV, 0)] * 2
                qraw_free = [(sV, 0)] * 2
                qbt_free = [(sHW, 0)] * 2
                psfree = [(sV, 0)] * 3
                psfreeDE = (sV, mV_base_snap)
                qh_store = [0] * NH
                for st in range(8):
                    for i in range(16):
                        plan_gather_slot(0, st * 16 + i)
                    for k in range(32):
                        xs = k % 2
                        ws = k % 2
                        mmXk = P.dma(
                            "gpsimd", sGQ, Xr[xs][:],
                            xT[k * 128:(k + 1) * 128, :],
                            waits=[xr_free[xs]])
                        mmWk = P.dma(
                            "gpsimd", sGQ, Wqr[ws][:],
                            wq[k * 128:(k + 1) * 128, st * 512:(st + 1) * 512],
                            waits=[wq_free[ws]])
                        w = [(sGQ, mmXk), (sGQ, mmWk)]
                        if k == 0:
                            w += [psfree[0], psfree[1], psfree[2], psfreeDE]
                        first, last = (k == 0), (k == 31)
                        for mm in range(4):
                            lhs = Wqr[ws][:, mm * 128:(mm + 1) * 128]
                            for half in range(2):
                                if mm < 3:
                                    out = [psA, psB, psC][mm][
                                        :, half * 512:(half + 1) * 512]
                                else:
                                    out = [psD, psE][half][:]
                                inc = sT if (mm == 3 and half == 1) else None
                                mk = P.top(
                                    lambda e, out=out, lhs=lhs, xs=xs,
                                    half=half, first=first, last=last:
                                    e.matmul(
                                        out, lhs,
                                        Xr[xs][:, half * 512:(half + 1) * 512],
                                        start=first, stop=last,
                                        skip_group_check=True),
                                    waits=w, inc=inc)
                                w = []
                        if k >= 1:
                            xr_free[(k - 1) % 2] = (sT, mk)
                            wq_free[(k - 1) % 2] = (sT, mk)
                    mT_sup = fence(psA[:, 0:1])
                    xr_free[31 % 2] = (sT, mT_sup)
                    wq_free[31 % 2] = (sT, mT_sup)

                    for mm in range(4):
                        h = st * 4 + mm
                        qs = mm % 2
                        wv = [(sT, mT_sup), qraw_free[qs]]
                        if st == 0 and mm == 0:
                            wv += [(sHW, mHW_cos), (sHW, mHW_sin)]
                        if mm < 3:
                            pst = [psA, psB, psC][mm]
                            mcp = P.chained(
                                "vector", sV,
                                lambda e, pst=pst, qs=qs: e.tensor_copy(
                                    qraw[qs][:], pst[:]), waits=wv)
                        else:
                            P.chained(
                                "vector", sV,
                                lambda e, qs=qs: e.tensor_copy(
                                    qraw[qs][:, 0:512], psD[:]), waits=wv)
                            mcp = P.chained(
                                "vector", sV,
                                lambda e, qs=qs: e.tensor_copy(
                                    qraw[qs][:, 512:1024], psE[:]))
                        if mm < 3:
                            psfree[mm] = (sV, mcp)
                        else:
                            psfreeDE = (sV, mcp)
                        P.chained("vector", sV, lambda e, qs=qs: e.tensor_copy(
                            shf[0:64, :], qraw[qs][64:128, :]))
                        P.chained("vector", sV, lambda e, qs=qs: e.tensor_copy(
                            shf[64:128, :], qraw[qs][0:64, :]))
                        P.chained("vector", sV, lambda e, qs=qs:
                                  e.tensor_tensor(t1r[:], qraw[qs][:],
                                                  cosT[:], ALU.mult))
                        mq2 = P.chained("vector", sV, lambda e, qs=qs:
                                        e.tensor_tensor(qraw[qs][:], shf[:],
                                                        sinT[:], ALU.mult))
                        qraw_free[qs] = (sV, mq2 + 1)
                        if h < 4:
                            dst = qga[:, h * 1024:(h + 1) * 1024]
                            mrp = P.chained(
                                "vector", sV,
                                lambda e, dst=dst, qs=qs: e.tensor_tensor(
                                    dst, t1r[:], qraw[qs][:], ALU.add))
                            qh_store[h] = 0
                        else:
                            bs = h % 2
                            mrp = P.chained(
                                "vector", sV,
                                lambda e, bs=bs, qs=qs: e.tensor_tensor(
                                    qbt[bs][:], t1r[:], qraw[qs][:], ALU.add),
                                waits=[qbt_free[bs]])
                            mst = P.dma("sync", sHW, qhbm[h], qbt[bs][:],
                                        waits=[(sV, mrp)])
                            qbt_free[bs] = (sHW, mst)
                            qh_store[h] = mst
                mV_rope_all = sV.n

            with ExitStack() as ph2:
                sb2 = lambda n, s, d: ph2.enter_context(
                    nc.sbuf_tensor(n, s, d))
                Pt = [sb2("Pt%d" % i, [128, 1024], BF16) for i in range(3)]
                ctxf = sb2("ctxf", [128, 1024], F32)

                sc_free = [psfree[0],
                           psfreeDE]
                ctx_free = psfree[1]
                rs_free = psfree[2]
                pt_cons = [(sT, 0)] * 3
                group_end = [(sT, 0)] * 9
                kbuf_free[0] = (sV, mV_rope_all)
                kbuf_free[1] = (sV, mV_rope_all)

                for g in range(8):
                    buf = g % 2
                    if g < 7:
                        for j in range(4):
                            h2 = (g + 1) * 4 + j
                            dst = qga[:, (((g + 1) % 2) * 4 + j) * 1024:
                                      (((g + 1) % 2) * 4 + j + 1) * 1024]
                            qh_store[h2] = P.dma(
                                "sync", sHW, dst, qhbm[h2],
                                waits=[(sHW, qh_store[h2]),
                                       group_end[g - 1] if g >= 1
                                       else (sT, 0)])
                    slot = [0]
                    for mm in range(4):
                        h = g * 4 + mm
                        qh = qga[:, ((g % 2) * 4 + mm) * 1024:
                                 ((g % 2) * 4 + mm + 1) * 1024]
                        head_w = [(sHW, qh_store[h]), gather_done[g],
                                  ktp_done[g], (sV, mV_rope_all)]
                        exp_done = [0] * NCH
                        for c in range(NCH + 1):
                            if c < NCH:
                                bufsc = c % 2
                                w = list(head_w) if c == 0 else []
                                w.append(sc_free[bufsc])
                                lhs = KTb[buf][:, c * 128:(c + 1) * 128]
                                if bufsc == 0:
                                    P.top(lambda e, lhs=lhs, qh=qh: e.matmul(
                                        psA[:, 0:512], lhs, qh[:, 0:512],
                                        start=True, stop=True,
                                        skip_group_check=True), waits=w)
                                    msc = P.top(
                                        lambda e, lhs=lhs, qh=qh: e.matmul(
                                            psA[:, 512:1024], lhs,
                                            qh[:, 512:1024], start=True,
                                            stop=True, skip_group_check=True),
                                        inc=sT)
                                else:
                                    P.top(lambda e, lhs=lhs, qh=qh: e.matmul(
                                        psD[:], lhs, qh[:, 0:512],
                                        start=True, stop=True,
                                        skip_group_check=True), waits=w)
                                    msc = P.top(
                                        lambda e, lhs=lhs, qh=qh: e.matmul(
                                            psE[:], lhs, qh[:, 512:1024],
                                            start=True, stop=True,
                                            skip_group_check=True), inc=sT)
                                pw = [(sT, msc), pt_cons[c % 3]]
                                if bufsc == 0:
                                    me = P.chained(
                                        "scalar", sS,
                                        lambda e, c=c: e.activation(
                                            Pt[c % 3][:], psA[:], AF.Exp),
                                        waits=pw)
                                else:
                                    P.chained(
                                        "scalar", sS,
                                        lambda e, c=c: e.activation(
                                            Pt[c % 3][:, 0:512], psD[:],
                                            AF.Exp), waits=pw)
                                    me = P.chained(
                                        "scalar", sS,
                                        lambda e, c=c: e.activation(
                                            Pt[c % 3][:, 512:1024], psE[:],
                                            AF.Exp))
                                exp_done[c] = me
                                sc_free[bufsc] = (sS, me)
                                if g < 7 and c % 2 == 0:
                                    plan_gather_slot(g + 1, slot[0])
                                    slot[0] += 1
                            if c >= 1:
                                cc = c - 1
                                w = [(sS, exp_done[cc])]
                                if cc == 0:
                                    w += [ctx_free, rs_free]
                                first, last = (cc == 0), (cc == NCH - 1)
                                vlhs = Vb[buf][:, cc * 128:(cc + 1) * 128]
                                pt = Pt[cc % 3]
                                P.top(lambda e, vlhs=vlhs, pt=pt, first=first,
                                      last=last: e.matmul(
                                          psB[:, 0:512], vlhs, pt[:, 0:512],
                                          start=first, stop=last,
                                          skip_group_check=True), waits=w)
                                P.top(lambda e, vlhs=vlhs, pt=pt, first=first,
                                      last=last: e.matmul(
                                          psB[:, 512:1024], vlhs,
                                          pt[:, 512:1024], start=first,
                                          stop=last, skip_group_check=True))
                                P.top(lambda e, pt=pt, first=first, last=last:
                                      e.matmul(
                                          psC[0:1, 0:512], ones_bf[:],
                                          pt[:, 0:512], start=first,
                                          stop=last, skip_group_check=True))
                                mrs = P.top(
                                    lambda e, pt=pt, first=first, last=last:
                                    e.matmul(
                                        psC[0:1, 512:1024], ones_bf[:],
                                        pt[:, 512:1024], start=first,
                                        stop=last, skip_group_check=True),
                                    inc=sT)
                                pt_cons[(cc - 1) % 3] = (sT, mrs)
                        mrs_done = (sT, fence(psC[:, 0:1]))
                        pt_cons[62 % 3] = mrs_done
                        pt_cons[63 % 3] = mrs_done
                        P.chained("scalar", sS, lambda e: e.activation(
                            r1[0:1, 0:512], psC[0:1, 0:512], AF.Ln),
                            waits=[mrs_done])
                        P.chained("scalar", sS, lambda e: e.activation(
                            r1[0:1, 512:1024], psC[0:1, 512:1024], AF.Ln))
                        P.chained("scalar", sS, lambda e: e.activation(
                            r2[0:1, 0:512], r1[0:1, 0:512], AF.Exp,
                            scale=cst[0:1, 2:3]))
                        mrin = P.chained("scalar", sS, lambda e: e.activation(
                            r2[0:1, 512:1024], r1[0:1, 512:1024], AF.Exp,
                            scale=cst[0:1, 2:3]))
                        rs_free = (sS, mrin)
                        P.top(lambda e: e.matmul(
                            psA[:, 0:512], onesf[:], r2[0:1, 0:512],
                            start=True, stop=True, skip_group_check=True),
                            waits=[(sS, mrin), sc_free[0]])
                        P.top(lambda e: e.matmul(
                            psA[:, 512:1024], onesf[:], r2[0:1, 512:1024],
                            start=True, stop=True, skip_group_check=True))
                        mbc = fence(psA[:, 0:1])
                        mcf = P.chained("vector", sV, lambda e: e.tensor_copy(
                            ctxf[:], psB[:]), waits=[(sT, mbc)])
                        ctx_free = (sV, mcf)
                        dst = ctxbf[:, h * 1024:(h + 1) * 1024]
                        mcb = P.chained(
                            "vector", sV,
                            lambda e, dst=dst: e.tensor_tensor(
                                dst, ctxf[:], psA[:], ALU.mult))
                        sc_free[0] = (sV, mcb)
                    group_end[g] = (sT, sT.n)
                    kbuf_free[buf] = (sT, sT.n)
                mV_attn_all = sV.n
                mS_attn_all = sS.n
                mT_attn_all = sT.n

            ph12.close()
            P.barrier(sems)
            with ExitStack() as ph3:
                sb3 = lambda n, s, d: ph3.enter_context(
                    nc.sbuf_tensor(n, s, d))
                WoS = [sb3("WoS%d" % i, [128, NK * 128], BF16)
                       for i in range(2)]
                xtr = [sb3("xtr%d" % i, [128, 1024], F32) for i in range(2)]
                s1t = [sb3("s1t%d" % i, [128, 1024], F32) for i in range(2)]
                sqb = [sb3("sqb%d" % i, [128, 1024], BF16) for i in range(2)]
                s1b = [sb3("s1b%d" % i, [128, 1024], BF16) for i in range(2)]

                wo_r = wo.rearrange("(k p) c -> p k c", p=128)
                wos_free = [(sV, 0)] * 2
                xtr_free = [(sV, 0)] * 2
                s1t_free = [(sHW, 0), (sHW, 0)]
                sq_free = [(sT, 0)] * 2
                s1b_free = [(sT, 0)] * 2
                ps_free = [(sV, 0)] * 3
                for m in range(NM):
                    sl = m % 2
                    mw = P.dma("gpsimd", sGQ, WoS[sl][:],
                               wo_r[:, :, m * 128:(m + 1) * 128],
                               waits=[wos_free[sl]])
                    mx = P.dma("sync", sHW, xtr[sl][:],
                               xT[m * 128:(m + 1) * 128, :],
                               waits=[xtr_free[sl]])
                    ps = [psA, psB, psC][m % 3]
                    w = [(sGQ, mw), ps_free[m % 3]]
                    for k in range(NK):
                        first, last = (k == 0), (k == NK - 1)
                        lhs = WoS[sl][:, k * 128:(k + 1) * 128]
                        rhs = ctxbf[:, k * 1024:(k + 1) * 1024]
                        P.top(lambda e, ps=ps, lhs=lhs, rhs=rhs, first=first,
                              last=last: e.matmul(
                                  ps[:, 0:512], lhs, rhs[:, 0:512],
                                  start=first, stop=last,
                                  skip_group_check=True), waits=w)
                        w = []
                        P.top(
                            lambda e, ps=ps, lhs=lhs, rhs=rhs, first=first,
                            last=last: e.matmul(
                                ps[:, 512:1024], lhs, rhs[:, 512:1024],
                                start=first, stop=last,
                                skip_group_check=True))
                    mwo = fence(ps[:, 0:1])
                    wos_free[sl] = (sT, mwo)
                    ms1 = P.chained(
                        "vector", sV,
                        lambda e, sl=sl, ps=ps: e.tensor_tensor(
                            s1t[sl][:], ps[:], xtr[sl][:], ALU.add),
                        waits=[(sT, mwo), (sHW, mx), s1t_free[sl],
                               sq_free[sl], s1b_free[sl]])
                    ps_free[m % 3] = (sV, ms1)
                    xtr_free[sl] = (sV, ms1)
                    msq = P.chained("scalar", sS,
                                    lambda e, sl=sl: e.activation(
                                        sqb[sl][:], s1t[sl][:], AF.Square),
                                    waits=[(sV, ms1)])
                    msb = P.chained("scalar", sS,
                                    lambda e, sl=sl: e.copy(
                                        s1b[sl][:], s1t[sl][:]))
                    first, last = (m == 0), (m == NM - 1)
                    P.top(lambda e, sl=sl, first=first, last=last: e.matmul(
                        psD[0:1, :], ones_bf[:], s1b[sl][:, 0:512],
                        start=first, stop=last, skip_group_check=True),
                        waits=[(sS, msb)])
                    P.top(lambda e, sl=sl, first=first, last=last: e.matmul(
                        psE[0:1, :], ones_bf[:], s1b[sl][:, 512:1024],
                        start=first, stop=last, skip_group_check=True))
                    P.top(lambda e, sl=sl, first=first, last=last: e.matmul(
                        psD[32:33, :], ones_bf[:], sqb[sl][:, 0:512],
                        start=first, stop=last, skip_group_check=True))
                    mst = P.top(
                        lambda e, sl=sl, first=first, last=last: e.matmul(
                            psE[32:33, :], ones_bf[:], sqb[sl][:, 512:1024],
                            start=first, stop=last, skip_group_check=True),
                        inc=sT)
                    sq_free[sl] = (sT, mst)
                    s1b_free[sl] = (sT, mst)
                    mss = P.dma("sync", sHW, s1h[m], s1t[sl][:],
                                waits=[(sV, ms1)])
                    s1t_free[sl] = (sHW, mss)
                fence(psD[:, 0:1])
                mT_stats1 = (sT, fence(psE[:, 0:1]))
                P.chained("vector", sV, lambda e: e.tensor_single_scalar(
                    r1[0:1, 0:512], psD[0:1, :], 1.0 / DM, ALU.mult),
                    waits=[mT_stats1])
                P.chained("vector", sV, lambda e: e.tensor_single_scalar(
                    r1[0:1, 512:1024], psE[0:1, :], 1.0 / DM, ALU.mult))
                P.chained("vector", sV, lambda e: e.tensor_tensor(
                    r2[0:1, :], r1[0:1, :], r1[0:1, :], ALU.mult))
                P.chained("vector", sV, lambda e: e.scalar_tensor_tensor(
                    r3[0:1, 0:512], psD[32:33, :], 1.0 / DM, r2[0:1, 0:512],
                    ALU.mult, ALU.subtract))
                mvar = P.chained("vector", sV, lambda e: e.scalar_tensor_tensor(
                    r3[0:1, 512:1024], psE[32:33, :], 1.0 / DM,
                    r2[0:1, 512:1024], ALU.mult, ALU.subtract))
                mln = P.chained("scalar", sS, lambda e: e.activation(
                    r2[0:1, :], r3[0:1, :], AF.Ln, bias=cst[0:1, 0:1]),
                    waits=[(sV, mvar)])
                mrstd = P.chained("scalar", sS, lambda e: e.activation(
                    r3[0:1, :], r2[0:1, :], AF.Exp, scale=cst[0:1, 1:2]))
                P.top(lambda e: e.matmul(psA[:, 0:512], onesf[:],
                                         r1[0:1, 0:512], start=True,
                                         stop=True, skip_group_check=True),
                      waits=[(sS, mrstd), ps_free[0], ps_free[1], ps_free[2]])
                P.top(lambda e: e.matmul(psA[:, 512:1024], onesf[:],
                                         r1[0:1, 512:1024], start=True,
                                         stop=True, skip_group_check=True))
                P.top(lambda e: e.matmul(psB[:, 0:512], onesf[:],
                                         r3[0:1, 0:512], start=True,
                                         stop=True, skip_group_check=True))
                P.top(lambda e: e.matmul(
                    psB[:, 512:1024], onesf[:], r3[0:1, 512:1024],
                    start=True, stop=True, skip_group_check=True))
                mbc2 = fence(psB[:, 0:1])
                P.chained("vector", sV, lambda e: e.tensor_copy(
                    mu_b[:], psA[:]), waits=[(sT, mbc2)])
                mmub = P.chained("vector", sV, lambda e: e.tensor_copy(
                    rstd_b[:], psB[:]))

            ctx13.close()
            P.barrier(sems)
            with ExitStack() as ph4:
                sb4 = lambda n, s, d: ph4.enter_context(
                    nc.sbuf_tensor(n, s, d))
                ATb = sb4("ATb", [128, NK * 128], BF16)
                Bpb = sb4("Bpb", [128, DM], BF16)
                Wrb = sb4("Wrb", [128, NK * 8], BF16)
                hsb = sb4("hsb", [128, 1024], F32)
                hwb = sb4("hwb", [128, 1024], BF16)
                lgT = sb4("lgT", [8, 1024], F32)
                cwT = sb4("cwT", [8, 1024], F32)
                CWs = sb4("CWs", [128, 64], F32)
                x1r = [sb4("x1r%d" % i, [128, 1024], BF16) for i in range(3)]
                s1rt = [sb4("s1rt%d" % i, [128, 1024], F32) for i in range(2)]
                x1sc = sb4("x1sc", [128, 1024], F32)
                WbS = [sb4("WbS%d" % i, [128, NK * 128], BF16)
                       for i in range(4)]
                s2t = [sb4("s2t%d" % i, [128, 1024], F32) for i in range(2)]
                sqb2 = [sb4("sqb2%d" % i, [128, 1024], BF16)
                        for i in range(2)]
                s2b = [sb4("s2b%d" % i, [128, 1024], BF16) for i in range(2)]

                atT_r = atT.rearrange("(k p) e -> p k e", p=128)
                wrT_r = wrT.rearrange("(k p) e -> p k e", p=128)
                mAT = P.dma("gpsimd", sGQ, ATb[:], atT_r)
                mBp = P.dma("gpsimd", sGQ, Bpb[:], bp)
                mWr = P.dma("gpsimd", sGQ, Wrb[:], wrT_r)

                x1_free = [(sT, 0)] * 3
                s1r_free = [(sV, 0)] * 2
                x1_store = [0] * NK
                for k in range(NK):
                    sl = k % 2
                    xs = k % 3
                    mrd = P.dma("sync", sHW, s1rt[sl][:], s1h[k],
                                waits=[s1r_free[sl]])
                    P.chained("vector", sV, lambda e, sl=sl: e.tensor_tensor(
                        x1sc[:], s1rt[sl][:], mu_b[:], ALU.subtract),
                        waits=[(sHW, mrd), (sV, mmub)])
                    mx1 = P.chained(
                        "vector", sV,
                        lambda e, xs=xs: e.tensor_tensor(
                            x1r[xs][:], x1sc[:], rstd_b[:], ALU.mult),
                        waits=[x1_free[xs]])
                    s1r_free[sl] = (sV, mx1)
                    x1_store[k] = P.dma("sync", sHW, x1bh[k], x1r[xs][:],
                                        waits=[(sV, mx1)])
                    first, last = (k == 0), (k == NK - 1)
                    w = [(sV, mx1)]
                    if k == 0:
                        w += [(sGQ, mAT), (sGQ, mWr)]
                    lhsA = ATb[:, k * 128:(k + 1) * 128]
                    lhsR = Wrb[:, k * 8:(k + 1) * 8]
                    P.top(lambda e, lhsA=lhsA, xs=xs, first=first, last=last:
                          e.matmul(psA[:, 0:512], lhsA, x1r[xs][:, 0:512],
                                   start=first, stop=last,
                                   skip_group_check=True), waits=w)
                    P.top(lambda e, lhsA=lhsA, xs=xs, first=first, last=last:
                          e.matmul(psA[:, 512:1024], lhsA,
                                   x1r[xs][:, 512:1024], start=first,
                                   stop=last, skip_group_check=True))
                    P.top(lambda e, lhsR=lhsR, xs=xs, first=first, last=last:
                          e.matmul(psB[0:8, 0:512], lhsR, x1r[xs][:, 0:512],
                                   start=first, stop=last,
                                   skip_group_check=True))
                    mk4 = P.top(
                        lambda e, lhsR=lhsR, xs=xs, first=first, last=last:
                        e.matmul(psB[0:8, 512:1024], lhsR,
                                 x1r[xs][:, 512:1024], start=first,
                                 stop=last, skip_group_check=True), inc=sT)
                    if k >= 1:
                        x1_free[(k - 1) % 3] = (sT, mk4)
                mT_4a = (sT, fence(psA[:, 0:1]))
                x1_free = [mT_4a] * 3

                P.barrier(sems)
                mlg = P.chained("vector", sV, lambda e: e.tensor_copy(
                    lgT[:], psB[0:8, :]), waits=[mT_4a])
                for qt in range(8):
                    src = lgT[0:8, qt * 128:(qt + 1) * 128]
                    P.top(lambda e, src=src, qt=qt: e.transpose(
                        psD[:, qt * 8:(qt + 1) * 8], src,
                        identf[0:8, 0:8]), waits=[(sV, mlg)])
                mtp = fence(psA[:, 0:1])
                P.barrier(sems)
                for qt in range(8):
                    lg = psD[:, qt * 8:(qt + 1) * 8]
                    w = [(sT, mtp)] if qt == 0 else []
                    P.chained("vector", sV, lambda e, lg=lg: e.reduce_max(
                        m1s[:], lg, axis=AX.X), waits=w)
                    P.chained("vector", sV, lambda e, lg=lg: e.tensor_scalar(
                        eq1[:], lg, m1s[:, 0:1], None, ALU.is_equal))
                    P.chained("vector", sV, lambda e, lg=lg:
                              e.scalar_tensor_tensor(t8[:], eq1[:], -1e30,
                                                     lg, ALU.mult, ALU.add))
                    P.chained("vector", sV, lambda e: e.reduce_max(
                        m2s[:], t8[:], axis=AX.X))
                    P.chained("vector", sV, lambda e: e.tensor_scalar(
                        eq2[:], t8[:], m2s[:, 0:1], None, ALU.is_equal))
                    mdv = P.chained("vector", sV, lambda e: e.tensor_tensor(
                        d21[:], m2s[:], m1s[:], ALU.subtract))
                    msx = P.chained("scalar", sS, lambda e: e.activation(
                        eds[:], d21[:], AF.Exp), waits=[(sV, mdv)])
                    P.chained("vector", sV, lambda e: e.tensor_single_scalar(
                        w1s[:], eds[:], 1.0, ALU.add), waits=[(sS, msx)])
                    P.chained("vector", sV, lambda e: e.reciprocal(
                        w1s[:], w1s[:]))
                    P.chained("vector", sV, lambda e: e.tensor_tensor(
                        w2s[:], eds[:], w1s[:], ALU.mult))
                    P.chained("vector", sV, lambda e: e.tensor_scalar(
                        t8[:], eq1[:], w1s[:, 0:1], None, ALU.mult))
                    P.chained("vector", sV, lambda e: e.tensor_scalar(
                        t8b[:], eq2[:], w2s[:, 0:1], None, ALU.mult))
                    dst = CWs[:, qt * 8:(qt + 1) * 8]
                    P.chained("vector", sV, lambda e, dst=dst:
                              e.tensor_tensor(dst, t8b[:], t8[:], ALU.add))
                mV_cw = sV.n
                P.barrier(sems)
                for qt in range(8):
                    src = CWs[:, qt * 8:(qt + 1) * 8]
                    pdst = psD if qt < 4 else psE
                    P.top(lambda e, src=src, qt=qt, pdst=pdst:
                          e.transpose(
                              pdst[0:8, (qt % 4) * 128:
                                   (qt % 4 + 1) * 128],
                              src, identf[:]),
                          waits=[(sV, mV_cw)] if qt == 0 else [])
                mtc = fence(psA[:, 0:1])
                P.barrier(sems)
                P.chained("vector", sV, lambda e: e.tensor_copy(
                    cwT[0:8, 0:512], psD[0:8, :]), waits=[(sT, mtc)])
                mcwT = P.chained("vector", sV, lambda e: e.tensor_copy(
                    cwT[0:8, 512:1024], psE[0:8, :]))
                P.top(lambda e: e.matmul(psC[:, 0:512], e16s[:],
                                         cwT[0:8, 0:512], start=True,
                                         stop=True, skip_group_check=True),
                      waits=[(sV, mcwT)])
                P.top(lambda e: e.matmul(
                    psC[:, 512:1024], e16s[:], cwT[0:8, 512:1024],
                    start=True, stop=True, skip_group_check=True))
                mcwx = fence(psC[:, 0:1])
                P.barrier(sems)
                mhs = P.chained("vector", sV, lambda e: e.tensor_copy(
                    hsb[:], psA[:]), waits=[mT_4a])
                mhw = P.chained("vector", sV, lambda e:
                                e.scalar_tensor_tensor(
                                    hwb[:], hsb[:], 2.0, psC[:],
                                    ALU.mult, ALU.mult),
                                waits=[(sT, mcwx)])

                P.barrier(sems)
                wbT_r = wbT.rearrange("(k p) c -> p k c", p=128)
                wb_free = [(sV, 0)] * 4
                s2_free = [(sHW, 0)] * 2
                sq2_free = [(sT, 0)] * 2
                s2b_free = [(sT, 0)] * 2
                ps4_free = [(sV, mhs), (sV, mhw), (sV, mhw)]
                mT_bp = (sGQ, mBp)
                sup_sizes = [3] * 10 + [2]
                mlist = []
                mcur = 0
                for ss in sup_sizes:
                    mlist.append(list(range(mcur, mcur + ss)))
                    mcur += ss
                for si, ms in enumerate(mlist):
                    P.barrier(sems)
                    for j, m in enumerate(ms):
                        sl4 = m % 4
                        mwb = P.dma("gpsimd", sGQ, WbS[sl4][:],
                                    wbT_r[:, :, m * 128:(m + 1) * 128],
                                    waits=[wb_free[sl4]])
                        wb_free[sl4] = (sGQ, mwb)
                    wmm = [0] * len(ms)
                    for k in range(NK):
                        xs = k % 3
                        mrd = P.dma("sync", sHW, x1r[xs][:], x1bh[k],
                                    waits=[x1_free[xs],
                                           (sHW, x1_store[k])])
                        w0 = [(sHW, mrd)]
                        if k == 0:
                            w0 += [ps4_free[j % 3] for j in range(len(ms))]
                        for j, m in enumerate(ms):
                            sl4 = m % 4
                            ps = [psA, psB, psC][j % 3]
                            first = (k == 0)
                            lhs = WbS[sl4][:, k * 128:(k + 1) * 128]
                            w = w0 if j == 0 else []
                            if k == 0:
                                w = list(w) + [wb_free[sl4]] \
                                    if j > 0 else w0 + [wb_free[sl4]]
                            P.top(lambda e, ps=ps, lhs=lhs, xs=xs,
                                  first=first: e.matmul(
                                      ps[:, 0:512], lhs, x1r[xs][:, 0:512],
                                      start=first, stop=False,
                                      skip_group_check=True), waits=w)
                            wmm[j] = P.top(
                                lambda e, ps=ps, lhs=lhs, xs=xs, first=first:
                                e.matmul(
                                    ps[:, 512:1024], lhs,
                                    x1r[xs][:, 512:1024], start=first,
                                    stop=False, skip_group_check=True),
                                inc=sT if j == len(ms) - 1 else None)
                        x1_free[xs] = (sT, wmm[-1])
                    for j, m in enumerate(ms):
                        sl = m % 2
                        sl4 = m % 4
                        ps = [psA, psB, psC][j % 3]
                        lhsB = Bpb[:, m * 128:(m + 1) * 128]
                        w = [(sV, mhw), mT_bp] if (si == 0 and j == 0) else []
                        P.top(lambda e, ps=ps, lhsB=lhsB: e.matmul(
                            ps[:, 0:512], lhsB, hwb[:, 0:512], start=False,
                            stop=True, skip_group_check=True), waits=w)
                        P.top(lambda e, ps=ps, lhsB=lhsB: e.matmul(
                            ps[:, 512:1024], lhsB, hwb[:, 512:1024],
                            start=False, stop=True, skip_group_check=True))
                        mlb = fence(ps[:, 0:1])
                        wb_free[sl4] = (sT, mlb)
                        mrd2 = P.dma("sync", sHW, s1rt[sl]
                                     [:], s1h[m], waits=[s1r_free[sl]])
                        P.chained("vector", sV,
                                  lambda e, sl=sl: e.tensor_tensor(
                                      x1sc[:], s1rt[sl][:], mu_b[:],
                                      ALU.subtract),
                                  waits=[(sHW, mrd2)])
                        P.chained("vector", sV, lambda e: e.tensor_tensor(
                            x1sc[:], x1sc[:], rstd_b[:], ALU.mult))
                        ms2 = P.chained(
                            "vector", sV,
                            lambda e, sl=sl, ps=ps, m=m:
                            e.scalar_tensor_tensor(
                                s2t[sl][:], ps[:], b_sb[:, m:m + 1],
                                x1sc[:], ALU.add, ALU.add),
                            waits=[(sT, mlb), s2_free[sl], sq2_free[sl],
                                   s2b_free[sl]])
                        ps4_free[j % 3] = (sV, ms2)
                        s1r_free[sl] = (sV, ms2)
                        P.chained("scalar", sS, lambda e, sl=sl: e.activation(
                            sqb2[sl][:], s2t[sl][:], AF.Square),
                            waits=[(sV, ms2)])
                        msb2 = P.chained("scalar", sS,
                                         lambda e, sl=sl: e.copy(
                                             s2b[sl][:], s2t[sl][:]))
                        first, last = (m == 0), (m == NM - 1)
                        P.top(lambda e, sl=sl, first=first, last=last:
                              e.matmul(psD[0:1, :], ones_bf[:],
                                       s2b[sl][:, 0:512], start=first,
                                       stop=last, skip_group_check=True),
                              waits=[(sS, msb2)])
                        P.top(lambda e, sl=sl, first=first, last=last:
                              e.matmul(psE[0:1, :], ones_bf[:],
                                       s2b[sl][:, 512:1024], start=first,
                                       stop=last, skip_group_check=True))
                        P.top(lambda e, sl=sl, first=first, last=last:
                              e.matmul(psD[32:33, :], ones_bf[:],
                                       sqb2[sl][:, 0:512], start=first,
                                       stop=last, skip_group_check=True))
                        mst2 = P.top(
                            lambda e, sl=sl, first=first, last=last:
                            e.matmul(psE[32:33, :], ones_bf[:],
                                     sqb2[sl][:, 512:1024], start=first,
                                     stop=last, skip_group_check=True),
                            inc=sT)
                        sq2_free[sl] = (sT, mst2)
                        s2b_free[sl] = (sT, mst2)
                        mss2 = P.dma("sync", sHW, s2h[m], s2t[sl][:],
                                     waits=[(sV, ms2)])
                        s2_free[sl] = (sHW, mss2)
                    x1_free = [(sT, mlb)] * 3
                fence(psD[:, 0:1])
                mT_stats2 = (sT, fence(psE[:, 0:1]))
                P.chained("vector", sV, lambda e: e.tensor_single_scalar(
                    r1[0:1, 0:512], psD[0:1, :], 1.0 / DM, ALU.mult),
                    waits=[mT_stats2])
                P.chained("vector", sV, lambda e: e.tensor_single_scalar(
                    r1[0:1, 512:1024], psE[0:1, :], 1.0 / DM, ALU.mult))
                P.chained("vector", sV, lambda e: e.tensor_tensor(
                    r2[0:1, :], r1[0:1, :], r1[0:1, :], ALU.mult))
                P.chained("vector", sV, lambda e: e.scalar_tensor_tensor(
                    r3[0:1, 0:512], psD[32:33, :], 1.0 / DM, r2[0:1, 0:512],
                    ALU.mult, ALU.subtract))
                mvar2 = P.chained("vector", sV,
                                  lambda e: e.scalar_tensor_tensor(
                                      r3[0:1, 512:1024], psE[32:33, :],
                                      1.0 / DM, r2[0:1, 512:1024],
                                      ALU.mult, ALU.subtract))
                P.chained("scalar", sS, lambda e: e.activation(
                    r2[0:1, :], r3[0:1, :], AF.Ln, bias=cst[0:1, 0:1]),
                    waits=[(sV, mvar2)])
                mrstd2 = P.chained("scalar", sS, lambda e: e.activation(
                    r3[0:1, :], r2[0:1, :], AF.Exp, scale=cst[0:1, 1:2]))
                P.top(lambda e: e.matmul(psA[:, 0:512], onesf[:],
                                         r1[0:1, 0:512], start=True,
                                         stop=True, skip_group_check=True),
                      waits=[(sS, mrstd2), ps4_free[0], ps4_free[1],
                             ps4_free[2]])
                P.top(lambda e: e.matmul(psA[:, 512:1024], onesf[:],
                                         r1[0:1, 512:1024], start=True,
                                         stop=True, skip_group_check=True))
                P.top(lambda e: e.matmul(psB[:, 0:512], onesf[:],
                                         r3[0:1, 0:512], start=True,
                                         stop=True, skip_group_check=True))
                P.top(lambda e: e.matmul(
                    psB[:, 512:1024], onesf[:], r3[0:1, 512:1024],
                    start=True, stop=True, skip_group_check=True))
                mbc3 = fence(psB[:, 0:1])
                P.chained("vector", sV, lambda e: e.tensor_copy(
                    mu_b[:], psA[:]), waits=[(sT, mbc3)])
                mmub2 = P.chained("vector", sV, lambda e: e.tensor_copy(
                    rstd_b[:], psB[:]))

            P.barrier(sems)
            with ExitStack() as ph5:
                sb5 = lambda n, s, d: ph5.enter_context(
                    nc.sbuf_tensor(n, s, d))
                s2r = [sb5("s2r%d" % i, [128, 1024], F32) for i in range(2)]
                yt = [sb5("yt%d" % i, [128, 1024], F32) for i in range(2)]
                ysc = sb5("ysc", [128, 1024], F32)
                s2r_free = [(sV, 0)] * 2
                yt_free = [(sHW, 0)] * 2
                for m in range(NM):
                    sl = m % 2
                    mrd = P.dma("sync", sHW, s2r[sl][:], s2h[m],
                                waits=[s2r_free[sl]])
                    P.chained("vector", sV, lambda e, sl=sl: e.tensor_tensor(
                        ysc[:], s2r[sl][:], mu_b[:], ALU.subtract),
                        waits=[(sHW, mrd), (sV, mmub2)])
                    my = P.chained("vector", sV, lambda e, sl=sl:
                                   e.tensor_tensor(yt[sl][:], ysc[:],
                                                   rstd_b[:], ALU.mult),
                                   waits=[yt_free[sl]])
                    s2r_free[sl] = (sV, my)
                    mys = P.dma("sync", sHW, yT[m * 128:(m + 1) * 128, :],
                                yt[sl][:], waits=[(sV, my)])
                    yt_free[sl] = (sHW, mys)
                final = [(sHW, sHW.n), (sGQ, sGQ.n), (sSD, sSD.n)]

                def fin(e):
                    for (s, v) in final:
                        e.wait_ge(s.h, v)
                P.emit("sync", fin)

        with ExitStack() as ectx:
            for s in sems:
                s.h = ectx.enter_context(nc.semaphore(s.name))
            blk = ectx.enter_context(nc.Block())

            blk.sync(lambda e: [f(e) for f in P.ops["sync"]])
            blk.gpsimd(lambda e: [f(e) for f in P.ops["gpsimd"]])
            blk.vector(lambda e: [f(e) for f in P.ops["vector"]])
            blk.scalar(lambda e: [f(e) for f in P.ops["scalar"]])
            blk.tensor(lambda e: [f(e) for f in P.ops["tensor"]])

    return nc


_CACHE = {}


def _get_nc():
    if "nc" not in _CACHE:
        _CACHE["nc"] = build_nc()
    return _CACHE["nc"]


def kernel(L0_attn_hidden_states, L0_attn_Wq, L0_attn_Wo, L0_attn_kv_cache,
           L0_attn_block_table, L0_ffn_W_base, L0_ffn_b_base,
           L0_ffn_W_router, L0_ffn_A, L0_ffn_B):
    x = np.asarray(L0_attn_hidden_states, np.float32)
    B, S, H = x.shape
    xT = np.ascontiguousarray(x.reshape(B * S, H).T)
    wq = np.ascontiguousarray(L0_attn_Wq, dtype=np.float32)
    wo = np.ascontiguousarray(L0_attn_Wo, dtype=np.float32)
    wbT = np.ascontiguousarray(np.asarray(L0_ffn_W_base, np.float32).T)
    kvf = np.ascontiguousarray(
        np.asarray(L0_attn_kv_cache, np.float32).reshape(262144, 128))
    bt = np.ascontiguousarray(L0_attn_block_table, dtype=np.int32)
    atT = np.ascontiguousarray(
        np.asarray(L0_ffn_A, np.float32).reshape(128, 4096).T)
    bp = np.ascontiguousarray(
        np.asarray(L0_ffn_B, np.float32).transpose(0, 2, 1).reshape(128, 4096))
    wrT = np.ascontiguousarray(np.asarray(L0_ffn_W_router, np.float32).T)
    bb = np.ascontiguousarray(L0_ffn_b_base, dtype=np.float32)

    inv = 1.0 / (10000.0 ** (np.arange(0, 128, 2, dtype=np.float64) / 128.0))
    s128 = 1.0 / np.sqrt(128.0)
    e16 = (np.arange(128)[None, :] // 16 ==
           np.arange(8)[:, None]).astype(np.float32)
    ovec = (np.arange(128) % 16).astype(np.float32).reshape(128, 1)

    nc = _get_nc()
    in_maps = []
    for core in range(NCORES):
        c0 = core * QL
        pos = (np.arange(c0, c0 + QL) % S).astype(np.float64)
        ang = pos[None, :] * np.concatenate([inv, inv])[:, None]
        cosT = (np.cos(ang) * s128).astype(np.float32)
        sgn = np.where(np.arange(128) < 64, -1.0, 1.0)[:, None]
        sinT = (np.sin(ang) * sgn * s128).astype(np.float32)
        in_maps.append({
            "xT": np.ascontiguousarray(xT[:, c0:c0 + QL]),
            "wq": wq, "wo": wo, "wbT": wbT, "kvf": kvf, "bt": bt,
            "atT": atT, "bp": bp, "wrT": wrT, "bb": bb,
            "cosT": cosT, "sinT": sinT, "e16": e16, "ovec": ovec,
        })

    res = run_bass_kernel_spmd(nc, in_maps, core_ids=list(range(NCORES)))
    yT = np.concatenate([np.asarray(r["yT"]) for r in res.results], axis=1)
    return np.ascontiguousarray(yT.T).reshape(B, S, H).astype(np.float32)
